# version 05
# baseline (speedup 1.0000x reference)
import numpy as np
import ml_dtypes
import concourse.bass as bass
import concourse.mybir as mybir
from concourse.bass_utils import run_bass_kernel_spmd
from contextlib import ExitStack

F32 = mybir.dt.float32
BF16 = mybir.dt.bfloat16
I32 = mybir.dt.int32
AF = mybir.ActivationFunctionType
OP = mybir.AluOpType

D = 4096
SEQ = 8192
NC = 8
NCH = 32
TT = 512
EPS = 1e-5



class DmaRing:

    def __init__(self, nc, es, name, nslots, inc=16):
        self.sems = [es.enter_context(nc.semaphore(f"{name}{i}")) for i in range(nslots)]
        self.n = nslots
        self.inc = inc

    def sem(self, k):
        return self.sems[k % self.n]

    def wait(self, eng, k):
        eng.wait_ge(self.sems[k % self.n], self.inc * (k // self.n + 1))


def build_l1():
    nc = bass.Bass("TRN2", target_bir_lowering=False, debug=False)
    w1f = nc.dram_tensor("w1f", [NCH, 128, NCH * 128], F32, kind="ExternalInput").ap()
    w2t = nc.dram_tensor("w2t", [NCH, 128, TT], F32, kind="ExternalInput").ap()
    msk = nc.dram_tensor("msk", [NCH, 128, TT], F32, kind="ExternalInput").ap()
    qt = nc.dram_tensor("qt", [NCH, 128, TT], I32, kind="ExternalInput").ap()
    scl = nc.dram_tensor("scl", [NCH, 128, TT], F32, kind="ExternalInput").ap()
    a_out = nc.dram_tensor("a_out", [NCH, 128, TT], BF16, kind="ExternalOutput").ap()
    w_out = nc.dram_tensor("w_out", [NCH, 128, TT], BF16, kind="ExternalOutput").ap()
    u_out = nc.dram_tensor("u_out", [1, TT], BF16, kind="ExternalOutput").ap()

    es = ExitStack()
    sb = lambda n, s, d: es.enter_context(nc.sbuf_tensor(n, s, d))
    sem = lambda n: es.enter_context(nc.semaphore(n))
    w2r = sb("w2r", [128, 4 * TT], F32)
    w2b = sb("w2b", [128, NCH * TT], BF16)
    w1r = sb("w1r", [128, 3 * NCH * 128], F32)
    w1b = sb("w1b", [128, 3 * NCH * 128], BF16)
    mskr = sb("mskr", [128, 2 * TT], F32)
    atl = sb("atl", [128, 2 * TT], BF16)
    qtr = sb("qtr", [128, 2 * TT], I32)
    qfr = sb("qfr", [128, 2 * TT], F32)
    sclr = sb("sclr", [128, 2 * TT], F32)
    wtl = sb("wtl", [128, NCH * TT], BF16)
    onesb = sb("onesb", [128, 1], BF16)
    urow = sb("urow", [1, TT], BF16)
    pa0 = es.enter_context(nc.psum_tensor("pa0", [128, TT], F32))
    pa1 = es.enter_context(nc.psum_tensor("pa1", [128, TT], F32))
    pu = es.enter_context(nc.psum_tensor("pu", [1, TT], F32))

    r_w2l = DmaRing(nc, es, "r_w2l", 4)
    r_w1l = DmaRing(nc, es, "r_w1l", 3)
    r_mkl = DmaRing(nc, es, "r_mkl", 2)
    r_ql = DmaRing(nc, es, "r_ql", 2, inc=32)
    r_ast = DmaRing(nc, es, "r_ast", 2)
    r_wst = DmaRing(nc, es, "r_wst", 2)
    s_w2b = sem("s_w2b"); s_w1b = sem("s_w1b"); s_mm = sem("s_mm"); s_aev = sem("s_aev")
    s_qf = sem("s_qf"); s_wt = sem("s_wt"); s_umm = sem("s_umm"); s_fin = sem("s_fin")

    with nc.Block() as block:

        @block.sync
        def _(sy):
            for k in range(NCH):
                if k >= 3:
                    sy.wait_ge(s_w1b, k - 2)
                sy.dma_start(out=w1r[:, (k % 3) * D:(k % 3 + 1) * D],
                             in_=w1f[k]).then_inc(r_w1l.sem(k), 16)
                if k >= 2:
                    sy.wait_ge(s_aev, k - 1)
                sy.dma_start(out=mskr[:, (k % 2) * TT:(k % 2 + 1) * TT],
                             in_=msk[k]).then_inc(r_mkl.sem(k), 16)

        @block.gpsimd
        def _(g):
            for k in range(NCH + 3):
                if k < NCH:
                    if k >= 4:
                        g.wait_ge(s_w2b, k - 3)
                    g.dma_start(out=w2r[:, (k % 4) * TT:(k % 4 + 1) * TT],
                                in_=w2t[k]).then_inc(r_w2l.sem(k), 16)
                if 1 <= k <= NCH:
                    kk = k - 1
                    r_ql.wait(g, kk)
                    if kk >= 2:
                        g.wait_ge(s_wt, kk - 1)
                    g.tensor_copy(qfr[:, (kk % 2) * TT:(kk % 2 + 1) * TT],
                                  qtr[:, (kk % 2) * TT:(kk % 2 + 1) * TT]).then_inc(s_qf, 1)
                if 2 <= k <= NCH + 1:
                    kk = k - 2
                    g.wait_ge(s_qf, kk + 1)
                    g.tensor_tensor(wtl[:, kk * TT:(kk + 1) * TT],
                                    qfr[:, (kk % 2) * TT:(kk % 2 + 1) * TT],
                                    sclr[:, (kk % 2) * TT:(kk % 2 + 1) * TT],
                                    op=OP.mult).then_inc(s_wt, 1)
                if 3 <= k <= NCH + 2:
                    kk = k - 3
                    g.wait_ge(s_wt, kk + 1)
                    g.dma_start(out=w_out[kk],
                                in_=wtl[:, kk * TT:(kk + 1) * TT]).then_inc(r_wst.sem(kk), 16)
                if k < NCH:
                    if k >= 2:
                        g.wait_ge(s_wt, k - 1)
                    g.dma_start(out=qtr[:, (k % 2) * TT:(k % 2 + 1) * TT],
                                in_=qt[k]).then_inc(r_ql.sem(k), 16)
                    g.dma_start(out=sclr[:, (k % 2) * TT:(k % 2 + 1) * TT],
                                in_=scl[k]).then_inc(r_ql.sem(k), 16)
            g.wait_ge(s_umm, NCH + 1)
            g.dma_start(out=u_out[:, :], in_=urow[:, :]).then_inc(s_fin, 16)
            r_ast.wait(g, NCH - 2)
            r_ast.wait(g, NCH - 1)
            r_wst.wait(g, NCH - 2)
            r_wst.wait(g, NCH - 1)
            g.wait_ge(s_fin, 16)

        @block.scalar
        def _(sc):
            for k in range(NCH):
                sc.wait_ge(s_aev, k + 1)
                sc.dma_start(out=a_out[k],
                             in_=atl[:, (k % 2) * TT:(k % 2 + 1) * TT]).then_inc(r_ast.sem(k), 16)

        @block.vector
        def _(v):
            v.memset(onesb[:, :], 1.0)
            for k in range(NCH):
                r_w2l.wait(v, k)
                v.tensor_copy(w2b[:, k * TT:(k + 1) * TT],
                              w2r[:, (k % 4) * TT:(k % 4 + 1) * TT]).then_inc(s_w2b, 1)
            for k in range(NCH):
                r_w1l.wait(v, k)
                if k >= 3:
                    v.wait_ge(s_mm, k - 2)
                v.tensor_copy(w1b[:, (k % 3) * D:(k % 3 + 1) * D],
                              w1r[:, (k % 3) * D:(k % 3 + 1) * D]).then_inc(s_w1b, 1)
                v.wait_ge(s_mm, k + 1)
                r_mkl.wait(v, k)
                if k >= 2:
                    r_ast.wait(v, k - 2)
                v.tensor_tensor(atl[:, (k % 2) * TT:(k % 2 + 1) * TT],
                                (pa0, pa1)[k % 2][:, :],
                                mskr[:, (k % 2) * TT:(k % 2 + 1) * TT],
                                op=OP.mult).then_inc(s_aev, 1)
            v.wait_ge(s_umm, NCH)
            v.tensor_copy(urow[:, :], pu[:, :]).then_inc(s_umm, 1)

        @block.tensor
        def _(t):
            for k in range(NCH):
                t.wait_ge(s_w1b, k + 1)
                if k >= 2:
                    t.wait_ge(s_aev, k - 1)
                bank = (pa0, pa1)[k % 2]
                w1k = w1b[:, (k % 3) * D:(k % 3 + 1) * D]
                inst = None
                for dc in range(NCH):
                    if k < 6:
                        t.wait_ge(s_w2b, dc + 1)
                    inst = t.matmul(bank[:, :],
                                    w1k[:, dc * 128:(dc + 1) * 128],
                                    w2b[:, dc * TT:(dc + 1) * TT],
                                    start=(dc == 0), stop=(dc == NCH - 1))
                inst.then_inc(s_mm, 1)
                t.wait_ge(s_wt, k + 1)
                t.matmul(pu[:, :], onesb[:, :], wtl[:, k * TT:(k + 1) * TT],
                         start=(k == 0), stop=(k == NCH - 1)).then_inc(s_umm, 1)
    es.close()
    return nc


def build_l2():
    nc = bass.Bass("TRN2", target_bir_lowering=False, debug=False)
    xT = nc.dram_tensor("xT", [NCH, 128, 1024], F32, kind="ExternalInput").ap()
    A2 = nc.dram_tensor("A2", [NCH, 128, NCH * 128], BF16, kind="ExternalInput").ap()
    W2 = nc.dram_tensor("W2", [NCH, 128, NCH * 128], BF16, kind="ExternalInput").ap()
    b2v = nc.dram_tensor("b2v", [128, NCH], F32, kind="ExternalInput").ap()
    biasv = nc.dram_tensor("biasv", [128, NCH], F32, kind="ExternalInput").ap()
    uvec = nc.dram_tensor("uvec", [1, D], BF16, kind="ExternalInput").ap()
    yT = nc.dram_tensor("yT", [NCH, 128, 1024], F32, kind="ExternalOutput").ap()

    es = ExitStack()
    sb = lambda n, s, d: es.enter_context(nc.sbuf_tensor(n, s, d))
    sem = lambda n: es.enter_context(nc.semaphore(n))
    xbf = sb("xbf", [128, NCH * TT], BF16)
    r1 = sb("r1", [128, NCH * TT], F32)
    r1b = sb("r1b", [128, NCH * TT], BF16)
    astr = sb("astr", [128, 2 * D], BF16)
    xfr1 = sb("xfr1", [128, 4 * TT], F32)
    xfr2 = sb("xfr2", [128, 2 * TT], F32)
    gt = sb("gt", [128, 2 * TT], F32)
    ttb = sb("ttb", [128, 4 * TT], F32)
    acc_s = sb("acc_s", [128, 2 * TT], F32)
    acc_q = sb("acc_q", [128, 2 * TT], F32)
    sqt = sb("sqt", [128, 2 * TT], F32)
    outr = sb("outr", [128, 3 * TT], F32)
    sbr = sb("sbr", [128, TT], F32)
    sbmr = sb("sbmr", [128, TT], F32)
    srow = sb("srow", [1, 3 * TT], F32)
    negmb = sb("negmb", [1, TT], BF16)
    rstd = sb("rstd", [1, TT], F32)
    epst = sb("epst", [1, 1], F32)
    onescol = sb("onescol", [128, 1], F32)
    onesrow = sb("onesrow", [1, 128], F32)
    ubrow = sb("ubrow", [1, D], BF16)
    b2c = sb("b2c", [128, NCH], F32)
    biasc = sb("biasc", [128, NCH], F32)
    pmm0 = es.enter_context(nc.psum_tensor("pmm0", [128, TT], F32))
    pmm1 = es.enter_context(nc.psum_tensor("pmm1", [128, TT], F32))
    ps_s = es.enter_context(nc.psum_tensor("ps_s", [1, TT], F32))
    ps_q = es.enter_context(nc.psum_tensor("ps_q", [1, TT], F32))
    pbc_r = es.enter_context(nc.psum_tensor("pbc_r", [128, TT], F32))
    pbc_mr = es.enter_context(nc.psum_tensor("pbc_mr", [128, TT], F32))
    pmm = (pmm0, pmm1)

    NT = 2

    r_strip = DmaRing(nc, es, "r_strip", 2)
    r_x1 = DmaRing(nc, es, "r_x1", 4)
    r_x2 = DmaRing(nc, es, "r_x2", 2)
    r_out = DmaRing(nc, es, "r_out", 3)
    s_xbf = sem("s_xbf"); s_mm1 = sem("s_mm1"); s_gelu = sem("s_gelu"); s_ep = sem("s_ep")
    s_sq = sem("s_sq"); s_acc = sem("s_acc"); s_mrg = sem("s_mrg"); s_stat = sem("s_stat")
    s_mean = sem("s_mean"); s_var = sem("s_var"); s_sqv = sem("s_sqv"); s_row = sem("s_row")
    s_bc = sem("s_bc"); s_sbc = sem("s_sbc"); s_ffn = sem("s_ffn"); s_prd = sem("s_prd")
    s_t1 = sem("s_t1"); s_r2 = sem("s_r2"); s_onorm = sem("s_onorm"); s_init = sem("s_init")
    s_vch = sem("s_vch"); s_acs = sem("s_acs"); s_t2 = sem("s_t2"); s_on1 = sem("s_on1")
    vch = [0]

    def strip_consumer(k):
        ti, r = divmod(k, 2 * NCH)
        half, c = divmod(r, NCH)
        return ti, half, c

    with nc.Block() as block:

        @block.sync
        def _(sy):
            for k in range(NT * 2 * NCH):
                if k >= 2:
                    ti, half, c = strip_consumer(k - 2)
                    sy.wait_ge((s_mm1, s_ffn)[half], ti * NCH + c + 1)
                ti, half, c = strip_consumer(k)
                src = (A2, W2)[half]
                sy.dma_start(out=astr[:, (k % 2) * D:(k % 2 + 1) * D],
                             in_=src[c]).then_inc(r_strip.sem(k), 16)

        @block.gpsimd
        def _(g):
            g.dma_start(out=b2c[:, :], in_=b2v[:, :]).then_inc(s_init, 16)
            g.dma_start(out=biasc[:, :], in_=biasv[:, :]).then_inc(s_init, 16)
            g.dma_start(out=ubrow[:, :], in_=uvec[:, :]).then_inc(s_init, 16)

            def xfr1_casts(ti):
                if ti > 0:
                    g.wait_ge(s_mm1, ti * NCH)
                for c in range(NCH + 3):
                    if c < NCH:
                        k = ti * NCH + c
                        if k >= 4:
                            g.wait_ge(s_xbf, k - 3)
                        g.dma_start(out=xfr1[:, (k % 4) * TT:(k % 4 + 1) * TT],
                                    in_=xT[c][:, ti * TT:(ti + 1) * TT]).then_inc(r_x1.sem(k), 16)
                    if ti > 0 and c >= 3:
                        cc = c - 3
                        k = ti * NCH + cc
                        r_x1.wait(g, k)
                        g.tensor_copy(xbf[:, cc * TT:(cc + 1) * TT],
                                      xfr1[:, (k % 4) * TT:(k % 4 + 1) * TT]).then_inc(s_xbf, 1)

            def xfr2_load(k, ti, c):
                if k >= 2:
                    g.wait_ge(s_ep, k - 1)
                g.dma_start(out=xfr2[:, (k % 2) * TT:(k % 2 + 1) * TT],
                            in_=xT[c][:, ti * TT:(ti + 1) * TT]).then_inc(r_x2.sem(k), 16)

            def out_store(ti, c):
                k = ti * NCH + c
                g.wait_ge(s_onorm, k + 1)
                g.dma_start(out=yT[c][:, ti * TT:(ti + 1) * TT],
                            in_=outr[:, (k % 3) * TT:(k % 3 + 1) * TT]).then_inc(r_out.sem(k), 16)

            xfr1_casts(0)
            for c in range(NCH):
                xfr2_load(c, 0, c)
            for ti in range(NT):
                if ti + 1 < NT:
                    xfr1_casts(ti + 1)
                    for c in range(NCH):
                        xfr2_load((ti + 1) * NCH + c, ti + 1, c)
                        out_store(ti, c)
                else:
                    for c in range(NCH):
                        out_store(ti, c)
            for k in range(max(0, NT * NCH - 3), NT * NCH):
                r_out.wait(g, k)
        @block.tensor
        def _(t):
            t.wait_ge(s_init, 48)

            def ln_stats(gph):
                t.wait_ge(s_mrg, gph)
                if gph >= 2:
                    t.wait_ge(s_mean, gph - 1)
                t.matmul(ps_s[:, :], onescol[:, :], acc_s[:, 0:TT], start=True, stop=True)
                t.matmul(ps_q[:, :], onescol[:, :], acc_q[:, 0:TT], start=True, stop=True).then_inc(s_stat, 1)

            def ln_bcasts(gph):
                t.wait_ge(s_row, gph)
                if gph >= 2:
                    t.wait_ge(s_sbc, gph - 1)
                t.matmul(pbc_r[:, :], onesrow[:, :], rstd[:, :], start=True, stop=True)
                t.matmul(pbc_mr[:, :], onesrow[:, :], srow[:, 2 * TT:3 * TT], start=True, stop=True).then_inc(s_bc, 1)

            for ti in range(NT):
                tb = ti * NCH
                g1 = 2 * ti + 1
                g2 = 2 * ti + 2
                t.wait_ge(s_xbf, tb + NCH)
                if ti > 0:
                    ln_stats(g1 - 1)
                for c in range(NCH):
                    k = ti * 2 * NCH + c
                    r_strip.wait(t, k)
                    if c >= 2:
                        t.wait_ge(s_gelu, tb + c - 1)
                    elif ti > 0:
                        t.wait_ge(s_prd, tb + c - 1)
                    bank = pmm[c % 2]
                    sl = astr[:, (k % 2) * D:(k % 2 + 1) * D]
                    inst = None
                    for s in range(NCH):
                        inst = t.matmul(bank[:, :], sl[:, s * 128:(s + 1) * 128],
                                        xbf[:, s * TT:(s + 1) * TT],
                                        start=(s == 0), stop=(s == NCH - 1))
                    inst.then_inc(s_mm1, 1)
                    if ti > 0 and c == 0:
                        ln_bcasts(g1 - 1)
                t.wait_ge(s_ep, tb + NCH)
                for c in range(NCH):
                    k = ti * 2 * NCH + NCH + c
                    r_strip.wait(t, k)
                    if c >= 2:
                        t.wait_ge(s_prd, tb + c - 1)
                    else:
                        t.wait_ge(s_gelu, tb + NCH - 1 + c)
                    bank = pmm[c % 2]
                    sl = astr[:, (k % 2) * D:(k % 2 + 1) * D]
                    for i in range(NCH):
                        t.matmul(bank[:, :], sl[:, i * 128:(i + 1) * 128],
                                 r1b[:, i * TT:(i + 1) * TT],
                                 start=(i == 0), stop=False)
                    if c == 0:
                        ln_stats(g1)
                    if c == 1:
                        ln_bcasts(g1)
                        t.wait_ge(s_var, g1)
                        t.matmul(pmm[0][:, :], ubrow[:, 0:128], negmb[:, :],
                                 start=False, stop=True).then_inc(s_ffn, 1)
                    if c >= 1:
                        t.matmul(pmm[c % 2][:, :], ubrow[:, c * 128:(c + 1) * 128], negmb[:, :],
                                 start=False, stop=True).then_inc(s_ffn, 1)
                if ti == NT - 1:
                    ln_stats(g2)
                    ln_bcasts(g2)
        @block.scalar
        def _(sc):
            for ti in range(NT):
                tb = ti * NCH
                g1 = 2 * ti + 1
                g2 = 2 * ti + 2
                for c in range(NCH + 1):
                    if c < NCH:
                        sc.wait_ge(s_mm1, tb + c + 1)
                        if tb + c >= 2:
                            sc.wait_ge(s_ep, tb + c - 1)
                        sc.activation(gt[:, (c % 2) * TT:(c % 2 + 1) * TT], pmm[c % 2][:, :],
                                      AF.Gelu, bias=b2c[:, c:c + 1]).then_inc(s_gelu, 1)
                    if c >= 1:
                        m = ti * 2 * NCH + (c - 1)
                        sc.wait_ge(s_ep, tb + c)
                        if m >= 2:
                            sc.wait_ge(s_acc, m - 1)
                        sc.square(sqt[:, ((c - 1) % 2) * TT:((c - 1) % 2 + 1) * TT],
                                  r1[:, (c - 1) * TT:c * TT]).then_inc(s_sq, 1)
                sc.wait_ge(s_stat, g1)
                sc.mul(srow[:, 0:TT], ps_s[:, :], 1.0 / D)
                sc.mul(srow[:, TT:2 * TT], ps_q[:, :], 1.0 / D).then_inc(s_mean, 1)
                sc.wait_ge(s_var, g1)
                sc.activation(rstd[:, :], rstd[:, :], AF.Sqrt, bias=epst[:, :]).then_inc(s_sqv, 1)
                for c in range(NCH + 1):
                    if c < NCH:
                        sc.wait_ge(s_t1, tb + c + 1)
                        sc.add(r1[:, c * TT:(c + 1) * TT], ttb[:, (c % 4) * TT:(c % 4 + 1) * TT],
                               biasc[:, c:c + 1]).then_inc(s_r2, 1)
                    if c >= 1:
                        m = ti * 2 * NCH + NCH + (c - 1)
                        sc.wait_ge(s_acc, m - 1)
                        sc.wait_ge(s_r2, tb + c)
                        sc.square(sqt[:, ((c - 1) % 2) * TT:((c - 1) % 2 + 1) * TT],
                                  r1[:, (c - 1) * TT:c * TT]).then_inc(s_sq, 1)
                sc.wait_ge(s_stat, g2)
                sc.mul(srow[:, 0:TT], ps_s[:, :], 1.0 / D)
                sc.mul(srow[:, TT:2 * TT], ps_q[:, :], 1.0 / D).then_inc(s_mean, 1)
                sc.wait_ge(s_var, g2)
                sc.activation(rstd[:, :], rstd[:, :], AF.Sqrt, bias=epst[:, :]).then_inc(s_sqv, 1)

        @block.vector
        def _(v):
            v.memset(onescol[:, :], 1.0)
            v.memset(onesrow[:, :], 1.0)
            v.memset(epst[:, :], EPS)

            def acc_step(m, src_ap, sq_slot):
                loc = m % NCH
                p = loc % 2
                first = loc < 2
                if loc == 0:
                    v.wait_ge(s_stat, m // NCH)
                if not first:
                    v.wait_ge(s_acs, m - 1)
                if first:
                    v.tensor_copy(acc_s[:, p * TT:(p + 1) * TT], src_ap).then_inc(s_acs, 1)
                else:
                    v.tensor_add(acc_s[:, p * TT:(p + 1) * TT], acc_s[:, p * TT:(p + 1) * TT],
                                 src_ap).then_inc(s_acs, 1)
                v.wait_ge(s_sq, m + 1)
                if not first:
                    v.wait_ge(s_acc, m - 1)
                if first:
                    v.tensor_copy(acc_q[:, p * TT:(p + 1) * TT],
                                  sqt[:, sq_slot * TT:(sq_slot + 1) * TT]).then_inc(s_acc, 1)
                else:
                    v.tensor_add(acc_q[:, p * TT:(p + 1) * TT], acc_q[:, p * TT:(p + 1) * TT],
                                 sqt[:, sq_slot * TT:(sq_slot + 1) * TT]).then_inc(s_acc, 1)

            def merge(gph):
                v.wait_ge(s_acs, gph * NCH)
                v.wait_ge(s_acc, gph * NCH)
                v.tensor_add(acc_s[:, 0:TT], acc_s[:, 0:TT], acc_s[:, TT:2 * TT])
                v.tensor_add(acc_q[:, 0:TT], acc_q[:, 0:TT], acc_q[:, TT:2 * TT]).then_inc(s_mrg, 1)

            def ln_rows(gph):
                n0 = vch[0]
                v.wait_ge(s_mean, gph)
                v.tensor_tensor(rstd[:, :], srow[:, 0:TT], srow[:, 0:TT], op=OP.mult).then_inc(s_vch, 1)
                v.tensor_scalar(negmb[:, :], srow[:, 0:TT], -1.0, None, op0=OP.mult)
                v.wait_ge(s_vch, n0 + 1)
                v.tensor_sub(rstd[:, :], srow[:, TT:2 * TT], rstd[:, :]).then_inc(s_var, 1)
                v.wait_ge(s_sqv, gph)
                v.tensor_copy(srow[:, 2 * TT:3 * TT], srow[:, 0:TT])
                v.reciprocal(rstd[:, :], rstd[:, :]).then_inc(s_vch, 1)
                v.wait_ge(s_vch, n0 + 2)
                v.tensor_tensor(srow[:, 2 * TT:3 * TT], srow[:, 2 * TT:3 * TT], rstd[:, :],
                                op=OP.mult).then_inc(s_row, 1)
                vch[0] = n0 + 2
                v.wait_ge(s_bc, gph)
                v.tensor_copy(sbr[:, :], pbc_r[:, :])
                v.tensor_copy(sbmr[:, :], pbc_mr[:, :]).then_inc(s_sbc, 1)

            for c in range(NCH):
                r_x1.wait(v, c)
                v.tensor_copy(xbf[:, c * TT:(c + 1) * TT],
                              xfr1[:, (c % 4) * TT:(c % 4 + 1) * TT]).then_inc(s_xbf, 1)
            for ti in range(NT):
                tb = ti * NCH
                g1 = 2 * ti + 1
                g2 = 2 * ti + 2
                for c in range(NCH + 1):
                    if ti > 0 and c < NCH:
                        kn = (ti - 1) * NCH + c
                        if kn >= 3:
                            r_out.wait(v, kn - 3)
                        v.tensor_tensor(outr[:, (kn % 3) * TT:(kn % 3 + 1) * TT],
                                        r1[:, c * TT:(c + 1) * TT], sbr[:, :], op=OP.mult).then_inc(s_on1, 1)
                        if c >= 1:
                            o2 = outr[:, ((kn - 1) % 3) * TT:((kn - 1) % 3 + 1) * TT]
                            v.tensor_sub(o2, o2, sbmr[:, :]).then_inc(s_onorm, 1)
                    if c < NCH:
                        k = tb + c
                        v.wait_ge(s_gelu, k + 1)
                        r_x2.wait(v, k)
                        if ti > 0:
                            v.wait_ge(s_on1, (ti - 1) * NCH + c + 1)
                        gtc = gt[:, (c % 2) * TT:(c % 2 + 1) * TT]
                        xfc = xfr2[:, (k % 2) * TT:(k % 2 + 1) * TT]
                        v.tensor_add(r1[:, c * TT:(c + 1) * TT], gtc, xfc)
                        v.tensor_add(r1b[:, c * TT:(c + 1) * TT], gtc, xfc).then_inc(s_ep, 1)
                    if ti > 0 and c == NCH:
                        kn = (ti - 1) * NCH + NCH - 1
                        o2 = outr[:, (kn % 3) * TT:(kn % 3 + 1) * TT]
                        v.tensor_sub(o2, o2, sbmr[:, :]).then_inc(s_onorm, 1)
                    if c >= 1:
                        m = ti * 2 * NCH + (c - 1)
                        v.wait_ge(s_ep, tb + c)
                        acc_step(m, r1[:, (c - 1) * TT:c * TT], (c - 1) % 2)
                merge(g1)
                ln_rows(g1)
                v.wait_ge(s_sbc, g1)
                for c in range(NCH + 3):
                    if c < NCH:
                        v.wait_ge(s_ffn, tb + c + 1)
                        if tb + c >= 4:
                            v.wait_ge(s_r2, tb + c - 3)
                        sl1 = ttb[:, (c % 4) * TT:(c % 4 + 1) * TT]
                        v.tensor_add(sl1, pmm[c % 2][:, :], r1[:, c * TT:(c + 1) * TT]).then_inc(s_prd, 1)
                    if 1 <= c <= NCH:
                        v.wait_ge(s_prd, tb + c)
                        sl2 = ttb[:, ((c - 1) % 4) * TT:((c - 1) % 4 + 1) * TT]
                        v.tensor_tensor(sl2, sl2, sbr[:, :], op=OP.mult).then_inc(s_t2, 1)
                    if 2 <= c <= NCH + 1:
                        v.wait_ge(s_t2, tb + c - 1)
                        sl3 = ttb[:, ((c - 2) % 4) * TT:((c - 2) % 4 + 1) * TT]
                        v.tensor_sub(sl3, sl3, sbmr[:, :]).then_inc(s_t1, 1)
                    if c >= 3:
                        m = ti * 2 * NCH + NCH + (c - 3)
                        v.wait_ge(s_r2, tb + c - 2)
                        acc_step(m, r1[:, (c - 3) * TT:(c - 2) * TT], (c - 3) % 2)
                merge(g2)
                ln_rows(g2)
                v.wait_ge(s_sbc, g2)
                if ti == NT - 1:
                    for c in range(NCH + 1):
                        if c < NCH:
                            k = tb + c
                            if k >= 3:
                                r_out.wait(v, k - 3)
                            v.tensor_tensor(outr[:, (k % 3) * TT:(k % 3 + 1) * TT],
                                            r1[:, c * TT:(c + 1) * TT], sbr[:, :], op=OP.mult).then_inc(s_on1, 1)
                        if c >= 1:
                            v.wait_ge(s_on1, tb + c)
                            o2 = outr[:, ((tb + c - 1) % 3) * TT:((tb + c - 1) % 3 + 1) * TT]
                            v.tensor_sub(o2, o2, sbmr[:, :]).then_inc(s_onorm, 1)
    es.close()
    return nc


def kernel(L0_attn_x, L0_attn_w1, L0_attn_w2, L0_attn_b2, L0_attn_sparse_mask,
           L0_ffn_qweight, L0_ffn_scales, L0_ffn_bias):
    x = np.asarray(L0_attn_x, dtype=np.float32).reshape(SEQ, D)
    w1 = np.ascontiguousarray(np.asarray(L0_attn_w1, dtype=np.float32))
    w2 = np.ascontiguousarray(np.asarray(L0_attn_w2, dtype=np.float32))
    b2 = np.ascontiguousarray(np.asarray(L0_attn_b2, dtype=np.float32))
    mask = np.ascontiguousarray(np.asarray(L0_attn_sparse_mask, dtype=np.float32))
    qw = np.ascontiguousarray(np.asarray(L0_ffn_qweight, dtype=np.int32))
    scales = np.asarray(L0_ffn_scales, dtype=np.float32).reshape(D, 64)
    bias = np.ascontiguousarray(np.asarray(L0_ffn_bias, dtype=np.float32))

    w1f = np.ascontiguousarray(w1.reshape(NCH, 128, NCH, 128).transpose(2, 1, 0, 3).reshape(NCH, 128, D))
    in1 = []
    for j in range(NC):
        tj = slice(TT * j, TT * (j + 1))
        w2t = np.ascontiguousarray(w2[tj, :].T.reshape(NCH, 128, TT))
        mj = np.ascontiguousarray(mask[:, tj].reshape(NCH, 128, TT))
        qtj = np.ascontiguousarray(qw[tj, :].T.reshape(NCH, 128, TT))
        sclj = np.ascontiguousarray(np.repeat(scales[tj, :].T, 64, axis=0).reshape(NCH, 128, TT))
        in1.append({"w1f": w1f, "w2t": w2t, "msk": mj, "qt": qtj, "scl": sclj})
    nc1 = build_l1()
    res1 = run_bass_kernel_spmd(nc1, in1, core_ids=list(range(NC)))

    a_full = np.concatenate([res1.results[j]["a_out"].reshape(D, TT) for j in range(NC)], axis=1)
    w_full = np.concatenate([res1.results[j]["w_out"].reshape(D, TT) for j in range(NC)], axis=1)
    u_full = np.concatenate([res1.results[j]["u_out"][0] for j in range(NC)]).reshape(1, D)

    A2 = np.ascontiguousarray(a_full.reshape(NCH, 128, NCH, 128).transpose(2, 1, 0, 3).reshape(NCH, 128, D))
    W2 = np.ascontiguousarray(w_full.reshape(NCH, 128, NCH, 128).transpose(2, 1, 0, 3).reshape(NCH, 128, D))

    b2c_h = np.ascontiguousarray(b2.reshape(NCH, 128).T)
    biasc_h = np.ascontiguousarray(bias.reshape(NCH, 128).T)
    xT = np.ascontiguousarray(x.T)
    in2 = []
    for j in range(NC):
        xtj = np.ascontiguousarray(xT[:, 1024 * j:1024 * (j + 1)].reshape(NCH, 128, 1024))
        in2.append({"xT": xtj, "A2": A2, "W2": W2, "b2v": b2c_h, "biasv": biasc_h, "uvec": u_full})
    nc2 = build_l2()
    res2 = run_bass_kernel_spmd(nc2, in2, core_ids=list(range(NC)))

    y = np.concatenate([res2.results[j]["yT"].reshape(D, 1024) for j in range(NC)], axis=1)
    return np.ascontiguousarray(y.T).reshape(1, SEQ, D)


# version 06
# speedup vs baseline: 1.1092x; 1.1092x over previous
import numpy as np
import ml_dtypes
import concourse.bass as bass
import concourse.mybir as mybir
from concourse.bass_utils import run_bass_kernel_spmd
from contextlib import ExitStack

F32 = mybir.dt.float32
BF16 = mybir.dt.bfloat16
I32 = mybir.dt.int32
AF = mybir.ActivationFunctionType
OP = mybir.AluOpType

D = 4096
SEQ = 8192
NC = 8
NCH = 32
TT = 512
EPS = 1e-5



class DmaRing:

    def __init__(self, nc, es, name, nslots, inc=16):
        self.sems = [es.enter_context(nc.semaphore(f"{name}{i}")) for i in range(nslots)]
        self.n = nslots
        self.inc = inc

    def sem(self, k):
        return self.sems[k % self.n]

    def wait(self, eng, k):
        eng.wait_ge(self.sems[k % self.n], self.inc * (k // self.n + 1))


def build_l1():
    nc = bass.Bass("TRN2", target_bir_lowering=False, debug=False)
    w1f = nc.dram_tensor("w1f", [NCH, 128, NCH * 128], F32, kind="ExternalInput").ap()
    w2t = nc.dram_tensor("w2t", [NCH, 128, TT], F32, kind="ExternalInput").ap()
    msk = nc.dram_tensor("msk", [NCH, 128, TT], F32, kind="ExternalInput").ap()
    qt = nc.dram_tensor("qt", [NCH, 128, TT], I32, kind="ExternalInput").ap()
    scl = nc.dram_tensor("scl", [NCH, 128, TT], F32, kind="ExternalInput").ap()
    a_out = nc.dram_tensor("a_out", [NCH, 128, TT], BF16, kind="ExternalOutput").ap()
    w_out = nc.dram_tensor("w_out", [NCH, 128, TT], BF16, kind="ExternalOutput").ap()
    u_out = nc.dram_tensor("u_out", [1, TT], BF16, kind="ExternalOutput").ap()

    es = ExitStack()
    sb = lambda n, s, d: es.enter_context(nc.sbuf_tensor(n, s, d))
    sem = lambda n: es.enter_context(nc.semaphore(n))
    w2r = sb("w2r", [128, 4 * TT], F32)
    w2b = sb("w2b", [128, NCH * TT], BF16)
    w1r = sb("w1r", [128, 3 * NCH * 128], F32)
    w1b = sb("w1b", [128, 3 * NCH * 128], BF16)
    mskr = sb("mskr", [128, 2 * TT], F32)
    atl = sb("atl", [128, 2 * TT], BF16)
    qtr = sb("qtr", [128, 2 * TT], I32)
    qfr = sb("qfr", [128, 2 * TT], F32)
    sclr = sb("sclr", [128, 2 * TT], F32)
    wtl = sb("wtl", [128, NCH * TT], BF16)
    onesb = sb("onesb", [128, 1], BF16)
    urow = sb("urow", [1, TT], BF16)
    pa0 = es.enter_context(nc.psum_tensor("pa0", [128, TT], F32))
    pa1 = es.enter_context(nc.psum_tensor("pa1", [128, TT], F32))
    pu = es.enter_context(nc.psum_tensor("pu", [1, TT], F32))

    r_w2l = DmaRing(nc, es, "r_w2l", 4)
    r_w1l = DmaRing(nc, es, "r_w1l", 3)
    r_mkl = DmaRing(nc, es, "r_mkl", 2)
    r_ql = DmaRing(nc, es, "r_ql", 2, inc=32)
    r_ast = DmaRing(nc, es, "r_ast", 2)
    r_wst = DmaRing(nc, es, "r_wst", 2)
    s_w2b = sem("s_w2b"); s_w1b = sem("s_w1b"); s_mm = sem("s_mm"); s_aev = sem("s_aev")
    s_qf = sem("s_qf"); s_wt = sem("s_wt"); s_umm = sem("s_umm"); s_fin = sem("s_fin")

    with nc.Block() as block:

        @block.sync
        def _(sy):
            for k in range(NCH):
                if k >= 3:
                    sy.wait_ge(s_w1b, k - 2)
                sy.dma_start(out=w1r[:, (k % 3) * D:(k % 3 + 1) * D],
                             in_=w1f[k]).then_inc(r_w1l.sem(k), 16)
                if k >= 2:
                    sy.wait_ge(s_aev, k - 1)
                sy.dma_start(out=mskr[:, (k % 2) * TT:(k % 2 + 1) * TT],
                             in_=msk[k]).then_inc(r_mkl.sem(k), 16)

        @block.gpsimd
        def _(g):
            for k in range(NCH):
                if k >= 4:
                    g.wait_ge(s_w2b, k - 3)
                g.dma_start(out=w2r[:, (k % 4) * TT:(k % 4 + 1) * TT],
                            in_=w2t[k]).then_inc(r_w2l.sem(k), 16)
            for k in range(NCH + 3):
                if 1 <= k <= NCH:
                    kk = k - 1
                    r_ql.wait(g, kk)
                    if kk >= 2:
                        g.wait_ge(s_wt, kk - 1)
                    g.tensor_copy(qfr[:, (kk % 2) * TT:(kk % 2 + 1) * TT],
                                  qtr[:, (kk % 2) * TT:(kk % 2 + 1) * TT]).then_inc(s_qf, 1)
                if 2 <= k <= NCH + 1:
                    kk = k - 2
                    g.wait_ge(s_qf, kk + 1)
                    g.tensor_tensor(wtl[:, kk * TT:(kk + 1) * TT],
                                    qfr[:, (kk % 2) * TT:(kk % 2 + 1) * TT],
                                    sclr[:, (kk % 2) * TT:(kk % 2 + 1) * TT],
                                    op=OP.mult).then_inc(s_wt, 1)
                if 3 <= k <= NCH + 2:
                    kk = k - 3
                    g.wait_ge(s_wt, kk + 1)
                    g.dma_start(out=w_out[kk],
                                in_=wtl[:, kk * TT:(kk + 1) * TT]).then_inc(r_wst.sem(kk), 16)
                if k < NCH:
                    if k >= 2:
                        g.wait_ge(s_wt, k - 1)
                    g.dma_start(out=qtr[:, (k % 2) * TT:(k % 2 + 1) * TT],
                                in_=qt[k]).then_inc(r_ql.sem(k), 16)
                    g.dma_start(out=sclr[:, (k % 2) * TT:(k % 2 + 1) * TT],
                                in_=scl[k]).then_inc(r_ql.sem(k), 16)
            g.wait_ge(s_umm, NCH + 1)
            g.dma_start(out=u_out[:, :], in_=urow[:, :]).then_inc(s_fin, 16)
            r_ast.wait(g, NCH - 2)
            r_ast.wait(g, NCH - 1)
            r_wst.wait(g, NCH - 2)
            r_wst.wait(g, NCH - 1)
            g.wait_ge(s_fin, 16)

        @block.scalar
        def _(sc):
            for k in range(NCH):
                sc.wait_ge(s_aev, k + 1)
                sc.dma_start(out=a_out[k],
                             in_=atl[:, (k % 2) * TT:(k % 2 + 1) * TT]).then_inc(r_ast.sem(k), 16)

        @block.vector
        def _(v):
            v.memset(onesb[:, :], 1.0)
            for k in range(NCH):
                r_w2l.wait(v, k)
                v.tensor_copy(w2b[:, k * TT:(k + 1) * TT],
                              w2r[:, (k % 4) * TT:(k % 4 + 1) * TT]).then_inc(s_w2b, 1)
            for k in range(NCH):
                r_w1l.wait(v, k)
                if k >= 3:
                    v.wait_ge(s_mm, k - 2)
                v.tensor_copy(w1b[:, (k % 3) * D:(k % 3 + 1) * D],
                              w1r[:, (k % 3) * D:(k % 3 + 1) * D]).then_inc(s_w1b, 1)
                v.wait_ge(s_mm, k + 1)
                r_mkl.wait(v, k)
                if k >= 2:
                    r_ast.wait(v, k - 2)
                v.tensor_tensor(atl[:, (k % 2) * TT:(k % 2 + 1) * TT],
                                (pa0, pa1)[k % 2][:, :],
                                mskr[:, (k % 2) * TT:(k % 2 + 1) * TT],
                                op=OP.mult).then_inc(s_aev, 1)
            v.wait_ge(s_umm, NCH)
            v.tensor_copy(urow[:, :], pu[:, :]).then_inc(s_umm, 1)

        @block.tensor
        def _(t):
            for k in range(NCH):
                t.wait_ge(s_w1b, k + 1)
                if k >= 2:
                    t.wait_ge(s_aev, k - 1)
                bank = (pa0, pa1)[k % 2]
                w1k = w1b[:, (k % 3) * D:(k % 3 + 1) * D]
                inst = None
                for dc in range(NCH):
                    if k < 6:
                        t.wait_ge(s_w2b, dc + 1)
                    inst = t.matmul(bank[:, :],
                                    w1k[:, dc * 128:(dc + 1) * 128],
                                    w2b[:, dc * TT:(dc + 1) * TT],
                                    start=(dc == 0), stop=(dc == NCH - 1))
                inst.then_inc(s_mm, 1)
                t.wait_ge(s_wt, k + 1)
                t.matmul(pu[:, :], onesb[:, :], wtl[:, k * TT:(k + 1) * TT],
                         start=(k == 0), stop=(k == NCH - 1)).then_inc(s_umm, 1)
    es.close()
    return nc


def build_l2():
    nc = bass.Bass("TRN2", target_bir_lowering=False, debug=False)
    xT = nc.dram_tensor("xT", [NCH, 128, 1024], F32, kind="ExternalInput").ap()
    A2 = nc.dram_tensor("A2", [NCH, 128, NCH * 128], BF16, kind="ExternalInput").ap()
    W2 = nc.dram_tensor("W2", [NCH, 128, NCH * 128], BF16, kind="ExternalInput").ap()
    b2v = nc.dram_tensor("b2v", [128, NCH], F32, kind="ExternalInput").ap()
    biasv = nc.dram_tensor("biasv", [128, NCH], F32, kind="ExternalInput").ap()
    uvec = nc.dram_tensor("uvec", [1, D], BF16, kind="ExternalInput").ap()
    yT = nc.dram_tensor("yT", [NCH, 128, 1024], F32, kind="ExternalOutput").ap()

    es = ExitStack()
    sb = lambda n, s, d: es.enter_context(nc.sbuf_tensor(n, s, d))
    sem = lambda n: es.enter_context(nc.semaphore(n))
    xbf = sb("xbf", [128, NCH * TT], BF16)
    r1 = sb("r1", [128, NCH * TT], F32)
    r1b = sb("r1b", [128, NCH * TT], BF16)
    astr = sb("astr", [128, 2 * D], BF16)
    xfr1 = sb("xfr1", [128, 4 * TT], F32)
    xfr2 = sb("xfr2", [128, 2 * TT], F32)
    gt = sb("gt", [128, 2 * TT], F32)
    ttb = sb("ttb", [128, 4 * TT], F32)
    acc_s = sb("acc_s", [128, 2 * TT], F32)
    acc_q = sb("acc_q", [128, 2 * TT], F32)
    sqt = sb("sqt", [128, 2 * TT], F32)
    outr = sb("outr", [128, 3 * TT], F32)
    sbr = sb("sbr", [128, TT], F32)
    sbmr = sb("sbmr", [128, TT], F32)
    srow = sb("srow", [1, 3 * TT], F32)
    negmb = sb("negmb", [1, TT], BF16)
    rstd = sb("rstd", [1, TT], F32)
    epst = sb("epst", [1, 1], F32)
    onescol = sb("onescol", [128, 1], F32)
    onesrow = sb("onesrow", [1, 128], F32)
    ubrow = sb("ubrow", [1, D], BF16)
    b2c = sb("b2c", [128, NCH], F32)
    biasc = sb("biasc", [128, NCH], F32)
    pmm0 = es.enter_context(nc.psum_tensor("pmm0", [128, TT], F32))
    pmm1 = es.enter_context(nc.psum_tensor("pmm1", [128, TT], F32))
    ps_s = es.enter_context(nc.psum_tensor("ps_s", [1, TT], F32))
    ps_q = es.enter_context(nc.psum_tensor("ps_q", [1, TT], F32))
    pbc_r = es.enter_context(nc.psum_tensor("pbc_r", [128, TT], F32))
    pbc_mr = es.enter_context(nc.psum_tensor("pbc_mr", [128, TT], F32))
    pmm = (pmm0, pmm1)

    NT = 2

    r_strip = DmaRing(nc, es, "r_strip", 2)
    r_x1 = DmaRing(nc, es, "r_x1", 4)
    r_x2 = DmaRing(nc, es, "r_x2", 2)
    r_out = DmaRing(nc, es, "r_out", 3)
    s_xbf = sem("s_xbf"); s_mm1 = sem("s_mm1"); s_gelu = sem("s_gelu"); s_ep = sem("s_ep")
    s_sq = sem("s_sq"); s_acc = sem("s_acc"); s_mrg = sem("s_mrg"); s_stat = sem("s_stat")
    s_mean = sem("s_mean"); s_var = sem("s_var"); s_sqv = sem("s_sqv"); s_row = sem("s_row")
    s_bc = sem("s_bc"); s_sbc = sem("s_sbc"); s_ffn = sem("s_ffn"); s_prd = sem("s_prd")
    s_t1 = sem("s_t1"); s_r2 = sem("s_r2"); s_onorm = sem("s_onorm"); s_init = sem("s_init")
    s_vch = sem("s_vch"); s_acs = sem("s_acs"); s_t2 = sem("s_t2"); s_on1 = sem("s_on1")
    vch = [0]

    def strip_consumer(k):
        ti, r = divmod(k, 2 * NCH)
        half, c = divmod(r, NCH)
        return ti, half, c

    with nc.Block() as block:

        @block.sync
        def _(sy):
            for k in range(NT * 2 * NCH):
                if k >= 2:
                    ti, half, c = strip_consumer(k - 2)
                    sy.wait_ge((s_mm1, s_ffn)[half], ti * NCH + c + 1)
                ti, half, c = strip_consumer(k)
                src = (A2, W2)[half]
                sy.dma_start(out=astr[:, (k % 2) * D:(k % 2 + 1) * D],
                             in_=src[c]).then_inc(r_strip.sem(k), 16)

        @block.gpsimd
        def _(g):
            g.dma_start(out=b2c[:, :], in_=b2v[:, :]).then_inc(s_init, 16)
            g.dma_start(out=biasc[:, :], in_=biasv[:, :]).then_inc(s_init, 16)
            g.dma_start(out=ubrow[:, :], in_=uvec[:, :]).then_inc(s_init, 16)

            def xfr1_casts(ti):
                if ti > 0:
                    g.wait_ge(s_mm1, ti * NCH)
                for c in range(NCH + 3):
                    if c < NCH:
                        k = ti * NCH + c
                        if k >= 4:
                            g.wait_ge(s_xbf, k - 3)
                        g.dma_start(out=xfr1[:, (k % 4) * TT:(k % 4 + 1) * TT],
                                    in_=xT[c][:, ti * TT:(ti + 1) * TT]).then_inc(r_x1.sem(k), 16)
                    if ti > 0 and c >= 3:
                        cc = c - 3
                        k = ti * NCH + cc
                        r_x1.wait(g, k)
                        g.tensor_copy(xbf[:, cc * TT:(cc + 1) * TT],
                                      xfr1[:, (k % 4) * TT:(k % 4 + 1) * TT]).then_inc(s_xbf, 1)

            def xfr2_load(k, ti, c):
                if k >= 2:
                    g.wait_ge(s_ep, k - 1)
                g.dma_start(out=xfr2[:, (k % 2) * TT:(k % 2 + 1) * TT],
                            in_=xT[c][:, ti * TT:(ti + 1) * TT]).then_inc(r_x2.sem(k), 16)

            def out_store(ti, c):
                k = ti * NCH + c
                g.wait_ge(s_onorm, k + 1)
                g.dma_start(out=yT[c][:, ti * TT:(ti + 1) * TT],
                            in_=outr[:, (k % 3) * TT:(k % 3 + 1) * TT]).then_inc(r_out.sem(k), 16)

            xfr1_casts(0)
            for c in range(NCH):
                xfr2_load(c, 0, c)
            for ti in range(NT):
                if ti + 1 < NT:
                    xfr1_casts(ti + 1)
                    for c in range(NCH):
                        xfr2_load((ti + 1) * NCH + c, ti + 1, c)
                        out_store(ti, c)
                else:
                    for c in range(NCH):
                        out_store(ti, c)
            for k in range(max(0, NT * NCH - 3), NT * NCH):
                r_out.wait(g, k)
        @block.tensor
        def _(t):
            t.wait_ge(s_init, 48)

            def ln_stats(gph):
                t.wait_ge(s_mrg, gph)
                if gph >= 2:
                    t.wait_ge(s_mean, gph - 1)
                t.matmul(ps_s[:, :], onescol[:, :], acc_s[:, 0:TT], start=True, stop=True)
                t.matmul(ps_q[:, :], onescol[:, :], acc_q[:, 0:TT], start=True, stop=True).then_inc(s_stat, 1)

            def ln_bcasts(gph):
                t.wait_ge(s_row, gph)
                if gph >= 2:
                    t.wait_ge(s_sbc, gph - 1)
                t.matmul(pbc_r[:, :], onesrow[:, :], rstd[:, :], start=True, stop=True)
                t.matmul(pbc_mr[:, :], onesrow[:, :], srow[:, 2 * TT:3 * TT], start=True, stop=True).then_inc(s_bc, 1)

            for ti in range(NT):
                tb = ti * NCH
                g1 = 2 * ti + 1
                g2 = 2 * ti + 2
                t.wait_ge(s_xbf, tb + NCH)
                if ti > 0:
                    ln_stats(g1 - 1)
                for c in range(NCH):
                    k = ti * 2 * NCH + c
                    r_strip.wait(t, k)
                    if c >= 2:
                        t.wait_ge(s_gelu, tb + c - 1)
                    elif ti > 0:
                        t.wait_ge(s_prd, tb + c - 1)
                    bank = pmm[c % 2]
                    sl = astr[:, (k % 2) * D:(k % 2 + 1) * D]
                    inst = None
                    for s in range(NCH):
                        inst = t.matmul(bank[:, :], sl[:, s * 128:(s + 1) * 128],
                                        xbf[:, s * TT:(s + 1) * TT],
                                        start=(s == 0), stop=(s == NCH - 1))
                    inst.then_inc(s_mm1, 1)
                    if ti > 0 and c == 0:
                        ln_bcasts(g1 - 1)
                t.wait_ge(s_ep, tb + NCH)
                for c in range(NCH):
                    k = ti * 2 * NCH + NCH + c
                    r_strip.wait(t, k)
                    if c >= 2:
                        t.wait_ge(s_prd, tb + c - 1)
                    else:
                        t.wait_ge(s_gelu, tb + NCH - 1 + c)
                    bank = pmm[c % 2]
                    sl = astr[:, (k % 2) * D:(k % 2 + 1) * D]
                    for i in range(NCH):
                        t.matmul(bank[:, :], sl[:, i * 128:(i + 1) * 128],
                                 r1b[:, i * TT:(i + 1) * TT],
                                 start=(i == 0), stop=False)
                    if c == 0:
                        ln_stats(g1)
                    if c == 1:
                        ln_bcasts(g1)
                        t.wait_ge(s_var, g1)
                        t.matmul(pmm[0][:, :], ubrow[:, 0:128], negmb[:, :],
                                 start=False, stop=True).then_inc(s_ffn, 1)
                    if c >= 1:
                        t.matmul(pmm[c % 2][:, :], ubrow[:, c * 128:(c + 1) * 128], negmb[:, :],
                                 start=False, stop=True).then_inc(s_ffn, 1)
                if ti == NT - 1:
                    ln_stats(g2)
                    ln_bcasts(g2)
        @block.scalar
        def _(sc):
            for ti in range(NT):
                tb = ti * NCH
                g1 = 2 * ti + 1
                g2 = 2 * ti + 2
                for c in range(NCH + 1):
                    if c < NCH:
                        sc.wait_ge(s_mm1, tb + c + 1)
                        if tb + c >= 2:
                            sc.wait_ge(s_ep, tb + c - 1)
                        sc.activation(gt[:, (c % 2) * TT:(c % 2 + 1) * TT], pmm[c % 2][:, :],
                                      AF.Gelu, bias=b2c[:, c:c + 1]).then_inc(s_gelu, 1)
                    if c >= 1:
                        m = ti * 2 * NCH + (c - 1)
                        sc.wait_ge(s_ep, tb + c)
                        if m >= 2:
                            sc.wait_ge(s_acc, m - 1)
                        sc.square(sqt[:, ((c - 1) % 2) * TT:((c - 1) % 2 + 1) * TT],
                                  r1[:, (c - 1) * TT:c * TT]).then_inc(s_sq, 1)
                sc.wait_ge(s_stat, g1)
                sc.mul(srow[:, 0:TT], ps_s[:, :], 1.0 / D)
                sc.mul(srow[:, TT:2 * TT], ps_q[:, :], 1.0 / D).then_inc(s_mean, 1)
                sc.wait_ge(s_var, g1)
                sc.activation(rstd[:, :], rstd[:, :], AF.Sqrt, bias=epst[:, :]).then_inc(s_sqv, 1)
                for c in range(NCH + 1):
                    if c < NCH:
                        sc.wait_ge(s_t1, tb + c + 1)
                        sc.add(r1[:, c * TT:(c + 1) * TT], ttb[:, (c % 4) * TT:(c % 4 + 1) * TT],
                               biasc[:, c:c + 1]).then_inc(s_r2, 1)
                    if c >= 1:
                        m = ti * 2 * NCH + NCH + (c - 1)
                        sc.wait_ge(s_acc, m - 1)
                        sc.wait_ge(s_r2, tb + c)
                        sc.square(sqt[:, ((c - 1) % 2) * TT:((c - 1) % 2 + 1) * TT],
                                  r1[:, (c - 1) * TT:c * TT]).then_inc(s_sq, 1)
                sc.wait_ge(s_stat, g2)
                sc.mul(srow[:, 0:TT], ps_s[:, :], 1.0 / D)
                sc.mul(srow[:, TT:2 * TT], ps_q[:, :], 1.0 / D).then_inc(s_mean, 1)
                sc.wait_ge(s_var, g2)
                sc.activation(rstd[:, :], rstd[:, :], AF.Sqrt, bias=epst[:, :]).then_inc(s_sqv, 1)

        @block.vector
        def _(v):
            v.memset(onescol[:, :], 1.0)
            v.memset(onesrow[:, :], 1.0)
            v.memset(epst[:, :], EPS)

            def acc_step(m, src_ap, sq_slot):
                loc = m % NCH
                p = loc % 2
                first = loc < 2
                if loc == 0:
                    v.wait_ge(s_stat, m // NCH)
                if not first:
                    v.wait_ge(s_acs, m - 1)
                if first:
                    v.tensor_copy(acc_s[:, p * TT:(p + 1) * TT], src_ap).then_inc(s_acs, 1)
                else:
                    v.tensor_add(acc_s[:, p * TT:(p + 1) * TT], acc_s[:, p * TT:(p + 1) * TT],
                                 src_ap).then_inc(s_acs, 1)
                v.wait_ge(s_sq, m + 1)
                if not first:
                    v.wait_ge(s_acc, m - 1)
                if first:
                    v.tensor_copy(acc_q[:, p * TT:(p + 1) * TT],
                                  sqt[:, sq_slot * TT:(sq_slot + 1) * TT]).then_inc(s_acc, 1)
                else:
                    v.tensor_add(acc_q[:, p * TT:(p + 1) * TT], acc_q[:, p * TT:(p + 1) * TT],
                                 sqt[:, sq_slot * TT:(sq_slot + 1) * TT]).then_inc(s_acc, 1)

            def merge(gph):
                v.wait_ge(s_acs, gph * NCH)
                v.wait_ge(s_acc, gph * NCH)
                v.tensor_add(acc_s[:, 0:TT], acc_s[:, 0:TT], acc_s[:, TT:2 * TT])
                v.tensor_add(acc_q[:, 0:TT], acc_q[:, 0:TT], acc_q[:, TT:2 * TT]).then_inc(s_mrg, 1)

            def ln_rows(gph):
                n0 = vch[0]
                v.wait_ge(s_mean, gph)
                v.tensor_tensor(rstd[:, :], srow[:, 0:TT], srow[:, 0:TT], op=OP.mult).then_inc(s_vch, 1)
                v.tensor_scalar(negmb[:, :], srow[:, 0:TT], -1.0, None, op0=OP.mult)
                v.wait_ge(s_vch, n0 + 1)
                v.tensor_sub(rstd[:, :], srow[:, TT:2 * TT], rstd[:, :]).then_inc(s_var, 1)
                v.wait_ge(s_sqv, gph)
                v.tensor_copy(srow[:, 2 * TT:3 * TT], srow[:, 0:TT])
                v.reciprocal(rstd[:, :], rstd[:, :]).then_inc(s_vch, 1)
                v.wait_ge(s_vch, n0 + 2)
                v.tensor_tensor(srow[:, 2 * TT:3 * TT], srow[:, 2 * TT:3 * TT], rstd[:, :],
                                op=OP.mult).then_inc(s_row, 1)
                vch[0] = n0 + 2
                v.wait_ge(s_bc, gph)
                v.tensor_copy(sbr[:, :], pbc_r[:, :])
                v.tensor_copy(sbmr[:, :], pbc_mr[:, :]).then_inc(s_sbc, 1)

            for c in range(NCH):
                r_x1.wait(v, c)
                v.tensor_copy(xbf[:, c * TT:(c + 1) * TT],
                              xfr1[:, (c % 4) * TT:(c % 4 + 1) * TT]).then_inc(s_xbf, 1)
            for ti in range(NT):
                tb = ti * NCH
                g1 = 2 * ti + 1
                g2 = 2 * ti + 2
                for c in range(NCH + 1):
                    if ti > 0 and c < NCH:
                        kn = (ti - 1) * NCH + c
                        if kn >= 3:
                            r_out.wait(v, kn - 3)
                        v.tensor_tensor(outr[:, (kn % 3) * TT:(kn % 3 + 1) * TT],
                                        r1[:, c * TT:(c + 1) * TT], sbr[:, :], op=OP.mult).then_inc(s_on1, 1)
                        if c >= 1:
                            o2 = outr[:, ((kn - 1) % 3) * TT:((kn - 1) % 3 + 1) * TT]
                            v.tensor_sub(o2, o2, sbmr[:, :]).then_inc(s_onorm, 1)
                    if c < NCH:
                        k = tb + c
                        v.wait_ge(s_gelu, k + 1)
                        r_x2.wait(v, k)
                        if ti > 0:
                            v.wait_ge(s_on1, (ti - 1) * NCH + c + 1)
                        gtc = gt[:, (c % 2) * TT:(c % 2 + 1) * TT]
                        xfc = xfr2[:, (k % 2) * TT:(k % 2 + 1) * TT]
                        v.tensor_add(r1[:, c * TT:(c + 1) * TT], gtc, xfc)
                        v.tensor_add(r1b[:, c * TT:(c + 1) * TT], gtc, xfc).then_inc(s_ep, 1)
                    if ti > 0 and c == NCH:
                        kn = (ti - 1) * NCH + NCH - 1
                        o2 = outr[:, (kn % 3) * TT:(kn % 3 + 1) * TT]
                        v.tensor_sub(o2, o2, sbmr[:, :]).then_inc(s_onorm, 1)
                    if c >= 1:
                        m = ti * 2 * NCH + (c - 1)
                        v.wait_ge(s_ep, tb + c)
                        acc_step(m, r1[:, (c - 1) * TT:c * TT], (c - 1) % 2)
                merge(g1)
                ln_rows(g1)
                v.wait_ge(s_sbc, g1)
                for c in range(NCH + 3):
                    if c < NCH:
                        v.wait_ge(s_ffn, tb + c + 1)
                        if tb + c >= 4:
                            v.wait_ge(s_r2, tb + c - 3)
                        sl1 = ttb[:, (c % 4) * TT:(c % 4 + 1) * TT]
                        v.tensor_add(sl1, pmm[c % 2][:, :], r1[:, c * TT:(c + 1) * TT]).then_inc(s_prd, 1)
                    if 1 <= c <= NCH:
                        v.wait_ge(s_prd, tb + c)
                        sl2 = ttb[:, ((c - 1) % 4) * TT:((c - 1) % 4 + 1) * TT]
                        v.tensor_tensor(sl2, sl2, sbr[:, :], op=OP.mult).then_inc(s_t2, 1)
                    if 2 <= c <= NCH + 1:
                        v.wait_ge(s_t2, tb + c - 1)
                        sl3 = ttb[:, ((c - 2) % 4) * TT:((c - 2) % 4 + 1) * TT]
                        v.tensor_sub(sl3, sl3, sbmr[:, :]).then_inc(s_t1, 1)
                    if c >= 3:
                        m = ti * 2 * NCH + NCH + (c - 3)
                        v.wait_ge(s_r2, tb + c - 2)
                        acc_step(m, r1[:, (c - 3) * TT:(c - 2) * TT], (c - 3) % 2)
                merge(g2)
                ln_rows(g2)
                v.wait_ge(s_sbc, g2)
                if ti == NT - 1:
                    for c in range(NCH + 1):
                        if c < NCH:
                            k = tb + c
                            if k >= 3:
                                r_out.wait(v, k - 3)
                            v.tensor_tensor(outr[:, (k % 3) * TT:(k % 3 + 1) * TT],
                                            r1[:, c * TT:(c + 1) * TT], sbr[:, :], op=OP.mult).then_inc(s_on1, 1)
                        if c >= 1:
                            v.wait_ge(s_on1, tb + c)
                            o2 = outr[:, ((tb + c - 1) % 3) * TT:((tb + c - 1) % 3 + 1) * TT]
                            v.tensor_sub(o2, o2, sbmr[:, :]).then_inc(s_onorm, 1)
    es.close()
    return nc


def kernel(L0_attn_x, L0_attn_w1, L0_attn_w2, L0_attn_b2, L0_attn_sparse_mask,
           L0_ffn_qweight, L0_ffn_scales, L0_ffn_bias):
    x = np.asarray(L0_attn_x, dtype=np.float32).reshape(SEQ, D)
    w1 = np.ascontiguousarray(np.asarray(L0_attn_w1, dtype=np.float32))
    w2 = np.ascontiguousarray(np.asarray(L0_attn_w2, dtype=np.float32))
    b2 = np.ascontiguousarray(np.asarray(L0_attn_b2, dtype=np.float32))
    mask = np.ascontiguousarray(np.asarray(L0_attn_sparse_mask, dtype=np.float32))
    qw = np.ascontiguousarray(np.asarray(L0_ffn_qweight, dtype=np.int32))
    scales = np.asarray(L0_ffn_scales, dtype=np.float32).reshape(D, 64)
    bias = np.ascontiguousarray(np.asarray(L0_ffn_bias, dtype=np.float32))

    w1f = np.ascontiguousarray(w1.reshape(NCH, 128, NCH, 128).transpose(2, 1, 0, 3).reshape(NCH, 128, D))
    in1 = []
    for j in range(NC):
        tj = slice(TT * j, TT * (j + 1))
        w2t = np.ascontiguousarray(w2[tj, :].T.reshape(NCH, 128, TT))
        mj = np.ascontiguousarray(mask[:, tj].reshape(NCH, 128, TT))
        qtj = np.ascontiguousarray(qw[tj, :].T.reshape(NCH, 128, TT))
        sclj = np.ascontiguousarray(np.repeat(scales[tj, :].T, 64, axis=0).reshape(NCH, 128, TT))
        in1.append({"w1f": w1f, "w2t": w2t, "msk": mj, "qt": qtj, "scl": sclj})
    nc1 = build_l1()
    res1 = run_bass_kernel_spmd(nc1, in1, core_ids=list(range(NC)))

    a_full = np.concatenate([res1.results[j]["a_out"].reshape(D, TT) for j in range(NC)], axis=1)
    w_full = np.concatenate([res1.results[j]["w_out"].reshape(D, TT) for j in range(NC)], axis=1)
    u_full = np.concatenate([res1.results[j]["u_out"][0] for j in range(NC)]).reshape(1, D)

    A2 = np.ascontiguousarray(a_full.reshape(NCH, 128, NCH, 128).transpose(2, 1, 0, 3).reshape(NCH, 128, D))
    W2 = np.ascontiguousarray(w_full.reshape(NCH, 128, NCH, 128).transpose(2, 1, 0, 3).reshape(NCH, 128, D))

    b2c_h = np.ascontiguousarray(b2.reshape(NCH, 128).T)
    biasc_h = np.ascontiguousarray(bias.reshape(NCH, 128).T)
    xT = np.ascontiguousarray(x.T)
    in2 = []
    for j in range(NC):
        xtj = np.ascontiguousarray(xT[:, 1024 * j:1024 * (j + 1)].reshape(NCH, 128, 1024))
        in2.append({"xT": xtj, "A2": A2, "W2": W2, "b2v": b2c_h, "biasv": biasc_h, "uvec": u_full})
    nc2 = build_l2()
    res2 = run_bass_kernel_spmd(nc2, in2, core_ids=list(range(NC)))

    y = np.concatenate([res2.results[j]["yT"].reshape(D, 1024) for j in range(NC)], axis=1)
    return np.ascontiguousarray(y.T).reshape(1, SEQ, D)
